# version 06
# baseline (speedup 1.0000x reference)
import sys
sys.path.insert(0, "/opt/trn_rl_repo")
from contextlib import ExitStack
import numpy as np
import concourse.bass as bass
import concourse.mybir as mybir
from concourse.bass_utils import run_bass_kernel_spmd

F32 = mybir.dt.float32
BF16 = mybir.dt.bfloat16
I32 = mybir.dt.int32
AF = mybir.ActivationFunctionType
AX = mybir.AxisListType
ALU = mybir.AluOpType

NCORE = 8
RANKS = 4
RG = [[0, 1, 2, 3], [4, 5, 6, 7]]
NSPLIT = 4


def cfg_full():
    return dict(D=4096, H=32, FF=11008, TPC=1024)


def cfg_tiny():
    return dict(D=512, H=4, FF=512, TPC=128)


def splits_of(n, k):
    base = n // k
    rem = n - base * k
    return [base + (1 if i < rem else 0) for i in range(k)]


def build_nc(cfg):
    D, H, FF, TPC = cfg["D"], cfg["H"], cfg["FF"], cfg["TPC"]
    DC = D // 128
    EB = D // 128
    HB = FF // 128
    TB = TPC // 2
    S = TPC * RANKS
    KT = S // 128
    KTL = TPC // 128
    LT = TPC // 128
    NQ = 2
    SP = splits_of(HB, NSPLIT)
    SO = [sum(SP[:i]) for i in range(NSPLIT)]
    SPMAX = max(SP)
    assert H == DC and TB <= 512 and D % 128 == 0 and FF % 128 == 0
    INV_SQRT_DK = float(1.0 / np.sqrt(128.0))
    EPS = 1e-5

    nc = bass.Bass("TRN2", target_bir_lowering=False)

    x_nat = nc.dram_tensor("x_nat", [TPC, D], F32, kind="ExternalInput").ap()
    xTd = nc.dram_tensor("xTd", [EB, 128, TPC], F32, kind="ExternalInput").ap()
    wq = nc.dram_tensor("wq", [EB, D, 128], F32, kind="ExternalInput").ap()
    wk = nc.dram_tensor("wk", [EB, D, 128], F32, kind="ExternalInput").ap()
    wv = nc.dram_tensor("wv", [EB, D, 128], F32, kind="ExternalInput").ap()
    wo = nc.dram_tensor("wo", [EB, D, 128], F32, kind="ExternalInput").ap()
    wg = nc.dram_tensor("wg", [HB, D, 128], I32, kind="ExternalInput").ap()
    wu = nc.dram_tensor("wu", [HB, D, 128], I32, kind="ExternalInput").ap()
    wd = nc.dram_tensor("wd", [EB, FF, 128], I32, kind="ExternalInput").ap()
    sg_d = nc.dram_tensor("sg", [1], F32, kind="ExternalInput").ap()
    su_d = nc.dram_tensor("su", [1], F32, kind="ExternalInput").ap()
    sd_d = nc.dram_tensor("sd", [1], F32, kind="ExternalInput").ap()
    outT = nc.dram_tensor("outT", [EB, 128, TPC], F32, kind="ExternalOutput").ap()

    ag_k_in = nc.dram_tensor("ag_k_in", [D, TPC], BF16, kind="Internal").ap()
    ag_v_in = nc.dram_tensor("ag_v_in", [D, TPC], BF16, kind="Internal").ap()
    agk_full = nc.dram_tensor("agk_full", [RANKS * D, TPC], BF16, kind="Internal").ap()
    agv_full = nc.dram_tensor("agv_full", [RANKS * D, TPC], BF16, kind="Internal").ap()
    xn_sp = nc.dram_tensor("xn_sp", [EB, 128, TPC], BF16, kind="Internal").ap()
    q_sp = nc.dram_tensor("q_sp", [EB, 128, TPC], BF16, kind="Internal").ap()
    y_sp = nc.dram_tensor("y_sp", [EB, 128, TPC], BF16, kind="Internal").ap()
    p_sp = nc.dram_tensor("p_sp", [EB, 128, TPC], F32, kind="Internal").ap()

    es = ExitStack()

    def sem(name):
        return es.enter_context(nc.semaphore(name))

    psum = es.enter_context(nc.psum_tensor("pspool", [128, 8, 512], F32)).ap()
    psum_bf = psum.bitcast(BF16)

    ident = es.enter_context(nc.sbuf_tensor("ident", [128, 128], BF16)).ap()
    flv = es.enter_context(nc.sbuf_tensor("flv", [1, 2], F32)).ap()
    fls = es.enter_context(nc.sbuf_tensor("fls", [1, 2], F32)).ap()
    scal = es.enter_context(nc.sbuf_tensor("scal", [128, 4], F32)).ap()
    onesk = es.enter_context(nc.sbuf_tensor("onesk", [128, 1], BF16)).ap()
    ones1 = es.enter_context(nc.sbuf_tensor("ones1", [1, 128], F32)).ap()

    xnT_st = ExitStack()
    xnT = xnT_st.enter_context(nc.sbuf_tensor("xnT", [128, DC, TPC], BF16)).ap()

    bar = sem("bar")
    bar_n = [0]
    blk = es.enter_context(nc.Block())
    ENGS = ["sync", "scalar", "vector", "tensor", "gpsimd"]

    def barrier():
        bar_n[0] += 1
        tgt = 5 * bar_n[0]

        @blk.vector
        def _(v):
            v.memset(flv[:, 0:1], 0.0).then_inc(bar, 1)
            v.wait_ge(bar, tgt)

        @blk.scalar
        def _(sc):
            sc.memzero(fls[:, 0:1]).then_inc(bar, 1)
            sc.wait_ge(bar, tgt)

        for e in ("sync", "tensor", "gpsimd"):
            @getattr(blk, e)
            def _(en):
                en.sem_inc(bar, 1)
                en.wait_ge(bar, tgt)

    s_init = sem("init")
    s_init2 = sem("init2")

    @blk.gpsimd
    def _(gp):
        gp.dma_start(out=scal[:, 0:1], in_=sg_d.unsqueeze(0).broadcast_to([128, 1])).then_inc(s_init, 16)
        gp.dma_start(out=scal[:, 2:3], in_=su_d.unsqueeze(0).broadcast_to([128, 1])).then_inc(s_init, 16)
        gp.dma_start(out=scal[:, 3:4], in_=sd_d.unsqueeze(0).broadcast_to([128, 1])).then_inc(s_init, 16)
        gp.memset(ident[:, :], 0.0).then_inc(s_init2, 1)
        gp.wait_ge(s_init, 48)
        gp.wait_ge(s_init2, 1)
        gp.affine_select(out=ident[:, :], in_=ident[:, :], compare_op=ALU.not_equal,
                         fill=1.0, base=0, pattern=[[-1, 128]], channel_multiplier=1).then_inc(s_init2, 1)
        gp.memset(onesk[:, :], 1.0).then_inc(s_init2, 1)
        gp.memset(ones1[:, :], 1.0).then_inc(s_init2, 1)
        gp.wait_ge(s_init2, 4)

    @blk.vector
    def _(v):
        v.wait_ge(s_init, 48)
        v.wait_ge(s_init2, 4)
        v.tensor_mul(scal[:, 1:2], scal[:, 2:3], scal[:, 3:4])

    barrier()

    ln_st = ExitStack()
    LNB = 4 if LT >= 4 else 2
    xtile = ln_st.enter_context(nc.sbuf_tensor("xtile", [128, LNB, D], F32)).ap()
    xnrow = ln_st.enter_context(nc.sbuf_tensor("xnrow", [128, LNB, D], BF16)).ap()
    sqj = ln_st.enter_context(nc.sbuf_tensor("sqj", [128, 2, D], BF16)).ap()
    stt = ln_st.enter_context(nc.sbuf_tensor("stt", [128, 2, 8], F32)).ap()

    s_xl = [sem(f"xl{i}") for i in range(LNB)]
    s_vch = sem("lnvch")
    s_v1, s_sq, s_s1, s_v2, s_ap, s_tr, s_te, s_sp = (
        sem("lnv1"), sem("lnsq"), sem("lns1"), sem("lnv2"),
        sem("lnap"), sem("lntr"), sem("lnte"), sem("lnsp"))

    GSZ = min(8, DC)
    NG = DC // GSZ

    @blk.sync
    def _(sy):
        for t in range(LT):
            if t >= LNB:
                sy.wait_ge(s_ap, t - LNB + 1)
            sy.dma_start(out=xtile[:, t % LNB, :], in_=x_nat[t * 128:(t + 1) * 128, :]).then_inc(s_xl[t % LNB], 16)
        sy.wait_ge(s_te, LT * NG)
        for eb in range(EB):
            sy.dma_start(out=xn_sp[eb], in_=xnT[:, eb, :]).then_inc(s_sp, 16)

    @blk.vector
    def _(v):
        nch = [0]

        def step(f):
            v.wait_ge(s_vch, nch[0])
            nch[0] += 1
            f().then_inc(s_vch, 1)
        for t in range(LT):
            v.wait_ge(s_xl[t % LNB], 16 * (t // LNB + 1))
            if t >= 2:
                v.wait_ge(s_ap, t - 1)
            xt = xtile[:, t % LNB, :]
            sp_ = stt[:, t % 2, :]
            step(lambda: v.reduce_sum(sp_[:, 0:1], xt, axis=AX.X))
            step(lambda: v.tensor_scalar_mul(sp_[:, 1:2], sp_[:, 0:1], 1.0 / D))
            v.wait_ge(s_sq, t + 1)
            step(lambda: v.tensor_mul(sp_[:, 3:4], sp_[:, 1:2], sp_[:, 1:2]))
            step(lambda: v.tensor_scalar_mul(sp_[:, 2:3], sp_[:, 2:3], 1.0 / D))
            step(lambda: v.tensor_sub(sp_[:, 3:4], sp_[:, 2:3], sp_[:, 3:4]))
            step(lambda: v.tensor_scalar_add(sp_[:, 3:4], sp_[:, 3:4], EPS))
            step(lambda: v.reciprocal(sp_[:, 4:5], sp_[:, 3:4]))
            v.wait_ge(s_s1, t + 1)
            step(lambda: v.tensor_mul(sp_[:, 6:7], sp_[:, 1:2], sp_[:, 5:6]))
            step(lambda: v.tensor_scalar_mul(sp_[:, 6:7], sp_[:, 6:7], -1.0))

    @blk.scalar
    def _(sc):
        for t in range(LT):
            sc.wait_ge(s_xl[t % LNB], 16 * (t // LNB + 1))
            if t >= 2:
                sc.wait_ge(s_vch, 9 * (t - 2) + 5)
            sc.activation(sqj[:, t % 2, :], xtile[:, t % LNB, :], AF.Square,
                          accum_out=stt[:, t % 2, 2:3]).then_inc(s_sq, 1)
            sc.wait_ge(s_vch, 9 * t + 7)
            sc.activation(stt[:, t % 2, 5:6], stt[:, t % 2, 4:5], AF.Sqrt).then_inc(s_s1, 1)
            sc.wait_ge(s_vch, 9 * t + 9)
            if t >= LNB:
                sc.wait_ge(s_tr, (t - LNB + 1) * NG)
            sc.activation(xnrow[:, t % LNB, :], xtile[:, t % LNB, :], AF.Identity,
                          bias=stt[:, t % 2, 6:7], scale=stt[:, t % 2, 5:6]).then_inc(s_ap, 1)
            for grp in range(NG):
                gi = t * NG + grp
                sc.wait_ge(s_tr, gi + 1)
                for j in range(GSZ):
                    dc = grp * GSZ + j
                    i = sc.copy(xnT[:, dc, t * 128:(t + 1) * 128],
                                psum_bf[:, gi % 2, j * 128:(j + 1) * 128])
                    if j == GSZ - 1:
                        i.then_inc(s_te, 1)

    @blk.tensor
    def _(tn):
        for t in range(LT):
            tn.wait_ge(s_ap, t + 1)
            for grp in range(NG):
                gi = t * NG + grp
                if gi >= 2:
                    tn.wait_ge(s_te, gi - 1)
                for j in range(GSZ):
                    dc = grp * GSZ + j
                    i = tn.transpose(psum_bf[:, gi % 2, j * 128:(j + 1) * 128],
                                     xnrow[:, t % LNB, dc * 128:(dc + 1) * 128], ident[:, :])
                    if j == GSZ - 1:
                        i.then_inc(s_tr, 1)

    barrier()
    ln_st.close()

    pj_st = ExitStack()
    PWB = 4
    wtf = pj_st.enter_context(nc.sbuf_tensor("wtf", [128, PWB, DC, 128], F32)).ap()
    wt = pj_st.enter_context(nc.sbuf_tensor("wt", [128, PWB, DC, 128], BF16)).ap()
    STGB = 8
    stg = pj_st.enter_context(nc.sbuf_tensor("stg", [128, STGB, TPC], BF16)).ap()

    s_wl = [sem(f"pwl{i}") for i in range(PWB)]
    s_wc = sem("pwc")
    s_mm = sem("pmm")
    s_ev = sem("pev")
    s_pst = [sem("pst0"), sem("pst1")]
    s_agk = sem("agk")
    s_agv = sem("agv")

    WCH = max(1, DC // 4)
    NW = (DC + WCH - 1) // WCH
    NCHK = min(4, EB, H)
    EBC = EB // NCHK
    HGC = H // NCHK

    kre = ag_k_in.rearrange("(eb p) t -> eb p t", p=128)
    vre = ag_v_in.rearrange("(eb p) t -> eb p t", p=128)
    agk_ch = agk_full.rearrange("(g r) t -> g r t", g=NCHK)
    agv_ch = agv_full.rearrange("(g r) t -> g r t", g=NCHK)
    projs = [("k", wk, kre), ("v", wv, vre), ("q", wq, q_sp)]
    NPJ = 3 * EB

    pst_cnt = [0, 0]
    pst_after = []
    for n in range(NPJ):
        pst_cnt[n % 2] += 16
        pst_after.append((pst_cnt[0], pst_cnt[1]))

    wsrcs = []
    for name, wsrc, dst in projs:
        for eb in range(EB):
            wsrcs.append((wsrc[eb], dst[eb]))

    def emit_wload(eng, n):
        if n >= PWB:
            eng.wait_ge(s_wc, n - PWB + 1)
        wr = wsrcs[n][0].rearrange("(dc p) e -> p dc e", p=128)
        for j in range(NW):
            eng.dma_start(out=wtf[:, n % PWB, j * WCH:(j + 1) * WCH, :],
                          in_=wr[:, j * WCH:(j + 1) * WCH, :]).then_inc(s_wl[n % PWB], 16)

    k_chunk_last = {(g + 1) * EBC - 1: g for g in range(NCHK)}
    v_chunk_last = {EB + (g + 1) * EBC - 1: g for g in range(NCHK)}

    @blk.gpsimd
    def _(gp):
        for m in range(NPJ):
            gp.wait_ge(s_ev, m + 1)
            gp.dma_start(out=wsrcs[m][1], in_=stg[:, m % STGB, :]).then_inc(s_pst[m % 2], 16)
            if m in k_chunk_last:
                g = k_chunk_last[m]
                gp.wait_ge(s_pst[0], pst_after[m][0])
                gp.wait_ge(s_pst[1], pst_after[m][1])
                gp.collective_compute("AllGather", ALU.bypass, replica_groups=RG,
                                      ins=[ag_k_in[g * EBC * 128:(g + 1) * EBC * 128, :]],
                                      outs=[agk_ch[g]]).then_inc(s_agk, 1)
            if m in v_chunk_last:
                g = v_chunk_last[m]
                gp.wait_ge(s_pst[0], pst_after[m][0])
                gp.wait_ge(s_pst[1], pst_after[m][1])
                gp.collective_compute("AllGather", ALU.bypass, replica_groups=RG,
                                      ins=[ag_v_in[g * EBC * 128:(g + 1) * EBC * 128, :]],
                                      outs=[agv_ch[g]]).then_inc(s_agv, 1)
        gp.wait_ge(s_pst[0], pst_after[NPJ - 1][0])
        gp.wait_ge(s_pst[1], pst_after[NPJ - 1][1])

    @blk.tensor
    def _(tn):
        for n in range(NPJ):
            tn.wait_ge(s_wc, n + 1)
            if n >= 2:
                tn.wait_ge(s_ev, n - 1)
            for dc in range(DC):
                first, last = dc == 0, dc == DC - 1
                b = (n % 2) * 2
                tn.matmul(psum[:, b, :TB], wt[:, n % PWB, dc, :], xnT[:, dc, 0:TB],
                          start=first, stop=last)
                i = tn.matmul(psum[:, b + 1, :TB], wt[:, n % PWB, dc, :], xnT[:, dc, TB:TPC],
                              start=first, stop=last)
                if last:
                    i.then_inc(s_mm, 1)

    @blk.vector
    def _(v):
        def cast(n):
            if n >= NPJ:
                return
            v.wait_ge(s_wl[n % PWB], 16 * NW * (n // PWB + 1))
            if n >= PWB:
                v.wait_ge(s_mm, n - PWB + 1)
            v.tensor_copy(wt[:, n % PWB, :, :], wtf[:, n % PWB, :, :]).then_inc(s_wc, 1)
        for k in range(PWB):
            cast(k)
        for n in range(NPJ):
            v.wait_ge(s_mm, n + 1)
            if n >= STGB:
                m = n - STGB
                v.wait_ge(s_pst[m % 2], 16 * (m // 2 + 1))
            b = (n % 2) * 2
            v.tensor_copy(stg[:, n % STGB, 0:TB], psum[:, b, :TB])
            v.tensor_copy(stg[:, n % STGB, TB:TPC], psum[:, b + 1, :TB]).then_inc(s_ev, 1)
            cast(n + PWB)

    @blk.sync
    def _(sy):
        for n in range(0, NPJ, 2):
            emit_wload(sy, n)
        sy.wait_ge(s_sp, 16 * EB)

    @blk.scalar
    def _(sc):
        for n in range(1, NPJ, 2):
            emit_wload(sc, n)

    barrier()
    pj_st.close()
    xnT_st.close()

    xnT2_st = ExitStack()
    xnT2 = xnT2_st.enter_context(nc.sbuf_tensor("xnT2", [128, DC, TPC], BF16)).ap()
    at_st1 = ExitStack()
    attnT = at_st1.enter_context(nc.sbuf_tensor("attnT", [128, DC, TPC], BF16)).ap()
    at_st = ExitStack()
    kt = at_st.enter_context(nc.sbuf_tensor("kt", [128, 2, S], BF16)).ap()
    VTB = 3
    vt = at_st.enter_context(nc.sbuf_tensor("vt", [128, VTB, KT, 128], BF16)).ap()
    qh = at_st.enter_context(nc.sbuf_tensor("qh", [128, 2, TPC], BF16)).ap()
    et = at_st.enter_context(nc.sbuf_tensor("et", [128, 2, NQ, TB], BF16)).ap()
    avf = at_st.enter_context(nc.sbuf_tensor("avf", [128, 2, NQ, TB], F32)).ap()
    rrow = at_st.enter_context(nc.sbuf_tensor("rrow", [1, 2, NQ, TB], F32)).ap()
    bcs = at_st.enter_context(nc.sbuf_tensor("bcs", [128, 2, NQ, TB], F32)).ap()
    rr_sp = nc.dram_tensor("rr_sp", [H, NQ, TB], F32, kind="Internal").ap()

    s_kl = [sem("akl0"), sem("akl1")]
    s_vl = [sem(f"avl{i}") for i in range(3)]
    s_ql = [sem("aql0"), sem("aql1")]
    s_smm = sem("asmm")
    s_exp = sem("aexp")
    s_dav = sem("adav")
    s_den = sem("aden")
    s_rc = sem("arc")
    s_rrs = [sem("arrs0"), sem("arrs1")]
    s_bcl = [sem("abcl0"), sem("abcl1")]
    s_avf = sem("aavf")
    s_aev = sem("aaev")
    s_x2 = sem("fx2")

    @blk.sync
    def _(sy):
        for h in range(H):
            if h % HGC == 0:
                g = h // HGC
                sy.wait_ge(s_agk, g + 1)
            if h >= 2:
                sy.wait_ge(s_smm, KT * (h - 1))
            g = h // HGC
            lr = (h % HGC) * 128
            for r in range(RANKS):
                sy.dma_start(out=kt[:, h % 2, r * TPC:(r + 1) * TPC],
                             in_=agk_ch[g][r * EBC * 128 + lr: r * EBC * 128 + lr + 128, :]).then_inc(s_kl[h % 2], 16)
            sy.dma_start(out=qh[:, h % 2, :], in_=q_sp[h]).then_inc(s_ql[h % 2], 16)
            if h >= 1:
                hp = h - 1
                sy.wait_ge(s_rc, hp + 1)
                sy.dma_start(out=rr_sp[hp], in_=rrow[:, hp % 2, :, :]).then_inc(s_rrs[hp % 2], 16)
                sy.wait_ge(s_rrs[hp % 2], 16 * (hp // 2 + 1))
                sy.dma_start(out=bcs[:, hp % 2, :, :],
                             in_=rr_sp[hp].unsqueeze(0).broadcast_to([128, NQ, TB])).then_inc(s_bcl[hp % 2], 16)
        hp = H - 1
        sy.wait_ge(s_rc, hp + 1)
        sy.dma_start(out=rr_sp[hp], in_=rrow[:, hp % 2, :, :]).then_inc(s_rrs[hp % 2], 16)
        sy.wait_ge(s_rrs[hp % 2], 16 * (hp // 2 + 1))
        sy.dma_start(out=bcs[:, hp % 2, :, :],
                     in_=rr_sp[hp].unsqueeze(0).broadcast_to([128, NQ, TB])).then_inc(s_bcl[hp % 2], 16)
        sy.wait_ge(s_sp, 16 * EB)
        for eb in range(EB):
            sy.dma_start(out=xnT2[:, eb, :], in_=xn_sp[eb]).then_inc(s_x2, 16)

    @blk.tensor
    def _(tn):
        def emit_avo(h, ktj, nk):
            ep = ktj % 2
            tn.wait_ge(s_exp, nk + 1)
            if ktj == 0 and h >= 1:
                tn.wait_ge(s_avf, h)
                tn.wait_ge(s_rc, h)
            vtile = vt[:, h % VTB, ktj, :]
            tn.matmul(psum[:, 4, :TB], vtile, et[:, ep, 0, :],
                      start=(ktj == 0), stop=(ktj == KT - 1))
            i = tn.matmul(psum[:, 5, :TB], vtile, et[:, ep, 1, :],
                          start=(ktj == 0), stop=(ktj == KT - 1))
            if ktj == KT - 1:
                i.then_inc(s_dav, 1)
            tn.matmul(psum[0:1, 6, :TB], onesk[:, :], et[:, ep, 0, :],
                      start=(ktj == 0), stop=(ktj == KT - 1))
            i2 = tn.matmul(psum[0:1, 7, :TB], onesk[:, :], et[:, ep, 1, :],
                           start=(ktj == 0), stop=(ktj == KT - 1))
            if ktj == KT - 1:
                i2.then_inc(s_den, 1)

        nk = 0
        for h in range(H):
            tn.wait_ge(s_kl[h % 2], 16 * RANKS * (h // 2 + 1))
            tn.wait_ge(s_vl[h % VTB], 16 * KT * (h // VTB + 1))
            tn.wait_ge(s_ql[h % 2], 16 * (h // 2 + 1))
            for ktj in range(KT):
                if nk >= 2:
                    tn.wait_ge(s_exp, nk - 1)
                bS = (ktj % 2) * 2
                ktile = kt[:, h % 2, ktj * 128:(ktj + 1) * 128]
                tn.matmul(psum[:, bS, :TB], ktile, qh[:, h % 2, 0:TB], start=True, stop=True)
                tn.matmul(psum[:, bS + 1, :TB], ktile, qh[:, h % 2, TB:TPC],
                          start=True, stop=True).then_inc(s_smm, 1)
                if nk >= 1:
                    ph_, pk_ = divmod(nk - 1, KT)
                    emit_avo(ph_, pk_, nk - 1)
                nk += 1
        ph_, pk_ = divmod(nk - 1, KT)
        emit_avo(ph_, pk_, nk - 1)

    def vt_loads(eng, h):
        g = h // HGC
        lr = (h % HGC) * 128
        eng.wait_ge(s_agv, g + 1)
        if h >= VTB:
            eng.wait_ge(s_dav, h - 2)
        for r in range(RANKS):
            for j in range(KTL):
                ktg = r * KTL + j
                eng.dma_start_transpose(
                    vt[:, h % VTB, ktg, :],
                    agv_ch[g][r * EBC * 128 + lr: r * EBC * 128 + lr + 128, j * 128:(j + 1) * 128],
                ).then_inc(s_vl[h % VTB], 16)

    @blk.scalar
    def _(sc):
        vt_loads(sc, 0)
        if H > 1:
            vt_loads(sc, 1)
        nk = 0
        for h in range(H):
            for ktj in range(KT):
                sc.wait_ge(s_smm, nk + 1)
                bS = (ktj % 2) * 2
                sc.activation(et[:, ktj % 2, :, :], psum[:, bS:bS + 2, :TB], AF.Exp,
                              scale=INV_SQRT_DK).then_inc(s_exp, 1)
                nk += 1
                if ktj == 2 and h + 2 < H:
                    vt_loads(sc, h + 2)


    @blk.vector
    def _(v):
        for h in range(H):
            v.wait_ge(s_dav, h + 1)
            if h >= 2:
                v.wait_ge(s_aev, h - 1)
            v.tensor_copy(avf[:, h % 2, 0, :], psum[:, 4, :TB])
            v.tensor_copy(avf[:, h % 2, 1, :], psum[:, 5, :TB]).then_inc(s_avf, 1)
            v.wait_ge(s_den, h + 1)
            if h >= 2:
                v.wait_ge(s_rrs[h % 2], 16 * ((h - 2) // 2 + 1))
            v.reciprocal(rrow[:, h % 2, :, :], psum[0:1, 6:8, :TB]).then_inc(s_rc, 1)
            v.wait_ge(s_bcl[h % 2], 16 * (h // 2 + 1))
            v.wait_ge(s_avf, h + 1)
            v.tensor_mul(attnT[:, h, 0:TB], avf[:, h % 2, 0, :], bcs[:, h % 2, 0, :])
            v.tensor_mul(attnT[:, h, TB:TPC], avf[:, h % 2, 1, :], bcs[:, h % 2, 1, :]).then_inc(s_aev, 1)

    barrier()
    at_st.close()

    wo_st = ExitStack()
    wt2 = wo_st.enter_context(nc.sbuf_tensor("wt2", [128, 2, DC, 128], BF16)).ap()
    ystg = wo_st.enter_context(nc.sbuf_tensor("ystg", [128, 2, TPC], BF16)).ap()

    s_wl2 = [sem("wwl0"), sem("wwl1")]
    s_mm2 = sem("wmm")
    s_ev2 = sem("wev")
    s_yst = [sem("yst0"), sem("yst1")]

    @blk.gpsimd
    def _(gp):
        for ob in range(EB):
            if ob >= 2:
                gp.wait_ge(s_mm2, ob - 1)
            wr = wo[ob].rearrange("(dc p) e -> p dc e", p=128)
            for j in range(NW):
                gp.dma_start(out=wt2[:, ob % 2, j * WCH:(j + 1) * WCH, :],
                             in_=wr[:, j * WCH:(j + 1) * WCH, :]).then_inc(s_wl2[ob % 2], 16)

    @blk.tensor
    def _(tn):
        for ob in range(EB):
            tn.wait_ge(s_wl2[ob % 2], 16 * NW * (ob // 2 + 1))
            if ob >= 2:
                tn.wait_ge(s_ev2, ob - 1)
            for ec in range(DC):
                first, last = ec == 0, ec == DC - 1
                b = (ob % 2) * 2
                tn.matmul(psum[:, b, :TB], wt2[:, ob % 2, ec, :], attnT[:, ec, 0:TB],
                          start=first, stop=last)
                i = tn.matmul(psum[:, b + 1, :TB], wt2[:, ob % 2, ec, :], attnT[:, ec, TB:TPC],
                              start=first, stop=last)
                if last:
                    i.then_inc(s_mm2, 1)

    @blk.vector
    def _(v):
        for ob in range(EB):
            v.wait_ge(s_mm2, ob + 1)
            if ob >= 2:
                v.wait_ge(s_yst[ob % 2], 16 * (ob // 2))
            b = (ob % 2) * 2
            v.tensor_copy(ystg[:, ob % 2, 0:TB], psum[:, b, :TB])
            v.tensor_copy(ystg[:, ob % 2, TB:TPC], psum[:, b + 1, :TB]).then_inc(s_ev2, 1)

    @blk.sync
    def _(sy):
        for ob in range(EB):
            sy.wait_ge(s_ev2, ob + 1)
            sy.dma_start(out=y_sp[ob], in_=ystg[:, ob % 2, :]).then_inc(s_yst[ob % 2], 16)
        sy.wait_ge(s_yst[0], 16 * ((EB + 1) // 2))
        sy.wait_ge(s_yst[1], 16 * (EB // 2))

    barrier()
    wo_st.close()
    at_st1.close()

    ff_st = ExitStack()
    mid = ff_st.enter_context(nc.sbuf_tensor("mid", [128, SPMAX, TPC], BF16)).ap()
    wgu = ff_st.enter_context(nc.sbuf_tensor("wgu", [128, 2, 2, DC, 128], BF16)).ap()
    sil = ff_st.enter_context(nc.sbuf_tensor("sil", [128, TPC], BF16)).ap()
    wdb = ff_st.enter_context(nc.sbuf_tensor("wdb", [128, 2, SPMAX, 128], BF16)).ap()
    asm = ff_st.enter_context(nc.sbuf_tensor("asm", [128, TPC], F32)).ap()
    xTc = ff_st.enter_context(nc.sbuf_tensor("xTc", [128, 2, TPC], F32)).ap()
    yc = ff_st.enter_context(nc.sbuf_tensor("yc", [128, 2, TPC], BF16)).ap()
    p0c = ff_st.enter_context(nc.sbuf_tensor("p0c", [128, 2, TPC], F32)).ap()
    otg = ff_st.enter_context(nc.sbuf_tensor("otg", [128, 2, TPC], F32)).ap()

    s_ach = sem("fach")
    s_gwl = [sem("fgw0"), sem("fgw1")]
    s_gmm = sem("fgmm")
    s_sil = sem("fsil")
    s_mde = sem("fmde")
    s_dwl = [sem("fdw0"), sem("fdw1")]
    s_dmm = sem("fdmm")
    s_dev = sem("fdev")
    s_pl = [sem("fpl0"), sem("fpl1")]
    s_al = [sem("fal0"), sem("fal1")]
    s_fst = [sem("fst0"), sem("fst1")]

    WCH2 = max(1, SPMAX // 3)
    dwl_cnt = [0, 0]
    dwl_exp = {}
    def dwn_chunks(spr):
        ch = []
        lo = 0
        while lo < spr:
            hi = min(lo + WCH2, spr)
            ch.append((lo, hi))
            lo = hi
        return ch

    @blk.gpsimd
    def _(gp):
        for r in range(NSPLIT):
            for j in range(SP[r]):
                n = SO[r] + j
                hb = n
                if n >= 2:
                    gp.wait_ge(s_gmm, n - 1)
                for wi, wsrc in ((0, wg), (1, wu)):
                    wr = wsrc[hb].rearrange("(dc p) h -> p dc h", p=128)
                    for jj in range(NW):
                        gp.dma_start(out=wgu[:, n % 2, wi, jj * WCH:(jj + 1) * WCH, :],
                                     in_=wr[:, jj * WCH:(jj + 1) * WCH, :]).then_inc(s_gwl[n % 2], 16)
            nchk = dwn_chunks(SP[r])
            for ob in range(EB):
                m = r * EB + ob
                if m >= 2:
                    gp.wait_ge(s_dmm, m - 1)
                wr = wd[ob].rearrange("(hc p) o -> p hc o", p=128)[:, SO[r]:SO[r] + SP[r], :]
                for (lo, hi) in nchk:
                    gp.dma_start(out=wdb[:, m % 2, lo:hi, :],
                                 in_=wr[:, lo:hi, :]).then_inc(s_dwl[m % 2], 16)
                dwl_cnt[m % 2] += 16 * len(nchk)
                dwl_exp[m] = dwl_cnt[m % 2]

    @blk.tensor
    def _(tn):
        tn.wait_ge(s_x2, 16 * EB)
        for r in range(NSPLIT):
            nchk = dwn_chunks(SP[r])
            if r >= 1:
                tn.wait_ge(s_dev, r * EB)
            for j in range(SP[r]):
                n = SO[r] + j
                tn.wait_ge(s_gwl[n % 2], 16 * 2 * NW * (n // 2 + 1))
                if n >= 2:
                    tn.wait_ge(s_mde, n - 1)
                bg = (n % 2) * 4
                for dc in range(DC):
                    first, last = dc == 0, dc == DC - 1
                    tn.matmul(psum[:, bg + 0, :TB], wgu[:, n % 2, 0, dc, :], xnT2[:, dc, 0:TB], start=first, stop=last)
                    tn.matmul(psum[:, bg + 1, :TB], wgu[:, n % 2, 0, dc, :], xnT2[:, dc, TB:TPC], start=first, stop=last)
                    tn.matmul(psum[:, bg + 2, :TB], wgu[:, n % 2, 1, dc, :], xnT2[:, dc, 0:TB], start=first, stop=last)
                    i = tn.matmul(psum[:, bg + 3, :TB], wgu[:, n % 2, 1, dc, :], xnT2[:, dc, TB:TPC], start=first, stop=last)
                    if last:
                        i.then_inc(s_gmm, 1)
            tn.wait_ge(s_mde, SO[r] + SP[r])
            for ob in range(EB):
                m = r * EB + ob
                tn.wait_ge(s_dwl[m % 2], dwl_exp[m])
                if m >= 2:
                    tn.wait_ge(s_dev, m - 1)
                bd = (m % 2) * 4
                for hc in range(SP[r]):
                    first, last = hc == 0, hc == SP[r] - 1
                    tn.matmul(psum[:, bd, :TB], wdb[:, m % 2, hc, :], mid[:, hc, 0:TB], start=first, stop=last)
                    i = tn.matmul(psum[:, bd + 1, :TB], wdb[:, m % 2, hc, :], mid[:, hc, TB:TPC], start=first, stop=last)
                    if last:
                        i.then_inc(s_dmm, 1)

    @blk.scalar
    def _(sc):
        for r in range(NSPLIT):
            for j in range(SP[r]):
                n = SO[r] + j
                sc.wait_ge(s_gmm, n + 1)
                sc.wait_ge(s_mde, n)
                bg = (n % 2) * 4
                sc.activation(sil[:, 0:TB], psum[:, bg + 0, :TB], AF.Silu, scale=scal[:, 0:1])
                sc.activation(sil[:, TB:TPC], psum[:, bg + 1, :TB], AF.Silu, scale=scal[:, 0:1]).then_inc(s_sil, 1)

    @blk.vector
    def _(v):
        for r in range(NSPLIT):
            for j in range(SP[r]):
                n = SO[r] + j
                v.wait_ge(s_sil, n + 1)
                if j == 0 and r >= 1:
                    v.wait_ge(s_dmm, r * EB)
                bg = (n % 2) * 4
                v.tensor_mul(mid[:, j, 0:TB], sil[:, 0:TB], psum[:, bg + 2, :TB])
                v.tensor_mul(mid[:, j, TB:TPC], sil[:, TB:TPC], psum[:, bg + 3, :TB]).then_inc(s_mde, 1)
            for ob in range(EB):
                m = r * EB + ob
                v.wait_ge(s_dmm, m + 1)
                v.wait_ge(s_fst[m % 2], 16 * (m // 2))
                b = (m % 2) * 4
                if r == 0:
                    v.tensor_copy(otg[:, m % 2, 0:TB], psum[:, b, :TB])
                    v.tensor_copy(otg[:, m % 2, TB:TPC], psum[:, b + 1, :TB]).then_inc(s_dev, 1)
                elif r < NSPLIT - 1:
                    v.wait_ge(s_pl[ob % 2], 16 * ((r - 1) * EB // 2 + ob // 2 + 1))
                    v.tensor_add(otg[:, m % 2, 0:TB], psum[:, b, :TB], p0c[:, ob % 2, 0:TB])
                    v.tensor_add(otg[:, m % 2, TB:TPC], psum[:, b + 1, :TB], p0c[:, ob % 2, TB:TPC]).then_inc(s_dev, 1)
                else:
                    v.wait_ge(s_pl[ob % 2], 16 * ((r - 1) * EB // 2 + ob // 2 + 1))
                    v.wait_ge(s_al[ob % 2], 32 * (ob // 2 + 1))
                    v.tensor_add(asm[:, 0:TB], psum[:, b, :TB], p0c[:, ob % 2, 0:TB]).then_inc(s_ach, 1)
                    v.tensor_add(asm[:, TB:TPC], psum[:, b + 1, :TB], p0c[:, ob % 2, TB:TPC]).then_inc(s_ach, 1)
                    v.wait_ge(s_ach, 3 * ob + 2)
                    v.scalar_tensor_tensor(asm[:, :], asm[:, :], scal[:, 1:2], xTc[:, ob % 2, :],
                                           op0=ALU.mult, op1=ALU.add).then_inc(s_ach, 1)
                    v.wait_ge(s_ach, 3 * ob + 3)
                    v.tensor_add(otg[:, m % 2, :], asm[:, :], yc[:, ob % 2, :]).then_inc(s_dev, 1)

    @blk.sync
    def _(sy):
        for r in range(NSPLIT):
            if r == NSPLIT - 1:
                sy.wait_ge(s_yst[0], 16 * ((EB + 1) // 2))
                sy.wait_ge(s_yst[1], 16 * (EB // 2))
            for ob in range(EB):
                m = r * EB + ob
                if r >= 1:
                    if ob == 0:
                        sy.wait_ge(s_fst[0], 16 * (r * EB // 2))
                        sy.wait_ge(s_fst[1], 16 * (r * EB // 2))
                    if m >= 2:
                        sy.wait_ge(s_dev, m - 1)
                    sy.dma_start(out=p0c[:, ob % 2, :], in_=p_sp[ob]).then_inc(s_pl[ob % 2], 16)
                    if r == NSPLIT - 1:
                        sy.dma_start(out=xTc[:, ob % 2, :], in_=xTd[ob]).then_inc(s_al[ob % 2], 16)
                        sy.dma_start(out=yc[:, ob % 2, :], in_=y_sp[ob]).then_inc(s_al[ob % 2], 16)
                sy.wait_ge(s_dev, m + 1)
                dst = outT[ob] if r == NSPLIT - 1 else p_sp[ob]
                sy.dma_start(out=dst, in_=otg[:, m % 2, :]).then_inc(s_fst[m % 2], 16)
        tot = NSPLIT * EB
        sy.wait_ge(s_fst[0], 16 * ((tot + 1) // 2))
        sy.wait_ge(s_fst[1], 16 * (tot // 2))

    ff_st.close()
    xnT2_st.close()
    es.close()
    return nc



def block_w(w, nblk):
    o, i = w.shape
    assert o == nblk * 128
    return np.ascontiguousarray(w.T.reshape(i, nblk, 128).transpose(1, 0, 2))


def make_in_maps(cfg, x, wq, wk, wv, wo_, wg_, wu_, wd_, sg, su, sd):
    D, TPC = cfg["D"], cfg["TPC"]
    EB = D // 128
    HB = cfg["FF"] // 128
    wqb = block_w(wq, EB)
    wkb = block_w(wk, EB)
    wvb = block_w(wv, EB)
    wob = block_w(wo_, EB)
    wgb = block_w(wg_, HB)
    wub = block_w(wu_, HB)
    wdb_ = block_w(wd_, EB)
    maps = []
    for c in range(NCORE):
        b = c // RANKS
        lo = (c % RANKS) * TPC
        xs = np.ascontiguousarray(x[b, lo:lo + TPC])
        xT = np.ascontiguousarray(xs.T).reshape(EB, 128, TPC)
        maps.append(dict(
            x_nat=xs, xTd=xT, wq=wqb, wk=wkb, wv=wvb, wo=wob,
            wg=wgb, wu=wub, wd=wdb_,
            sg=np.asarray([sg], np.float32), su=np.asarray([su], np.float32),
            sd=np.asarray([sd], np.float32)))
    return maps


def assemble_out(cfg, results):
    D, TPC = cfg["D"], cfg["TPC"]
    S = TPC * RANKS
    out = np.empty((2, S, D), np.float32)
    for c in range(NCORE):
        b = c // RANKS
        lo = (c % RANKS) * TPC
        oT = results[c]["outT"].reshape(D, TPC)
        out[b, lo:lo + TPC] = oT.T
    return out




_CFG = cfg_full()
_NC = None


def _get_nc():
    global _NC
    if _NC is None:
        _NC = build_nc(_CFG)
    return _NC


def kernel(L0_attn_x, L0_attn_w_q, L0_attn_w_k, L0_attn_w_v, L0_attn_w_o,
           L0_ffn_w_gate, L0_ffn_w_up, L0_ffn_w_down,
           L0_ffn_scale_gate, L0_ffn_scale_up, L0_ffn_scale_down):
    cfg = _CFG
    x = np.asarray(L0_attn_x, dtype=np.float32)
    maps = make_in_maps(
        cfg, x,
        np.asarray(L0_attn_w_q, dtype=np.float32),
        np.asarray(L0_attn_w_k, dtype=np.float32),
        np.asarray(L0_attn_w_v, dtype=np.float32),
        np.asarray(L0_attn_w_o, dtype=np.float32),
        np.asarray(L0_ffn_w_gate, dtype=np.int32),
        np.asarray(L0_ffn_w_up, dtype=np.int32),
        np.asarray(L0_ffn_w_down, dtype=np.int32),
        float(np.asarray(L0_ffn_scale_gate)),
        float(np.asarray(L0_ffn_scale_up)),
        float(np.asarray(L0_ffn_scale_down)))
    nc = _get_nc()
    res = run_bass_kernel_spmd(nc, maps, core_ids=list(range(NCORE)))
    return assemble_out(cfg, [r for r in res.results])


# version 07
# speedup vs baseline: 1.1405x; 1.1405x over previous
import sys
sys.path.insert(0, "/opt/trn_rl_repo")
from contextlib import ExitStack
import numpy as np
import concourse.bass as bass
import concourse.mybir as mybir
from concourse.bass_utils import run_bass_kernel_spmd

F32 = mybir.dt.float32
BF16 = mybir.dt.bfloat16
I32 = mybir.dt.int32
AF = mybir.ActivationFunctionType
AX = mybir.AxisListType
ALU = mybir.AluOpType

NCORE = 8
RANKS = 4
RG = [[0, 1, 2, 3], [4, 5, 6, 7]]
NSPLIT = 4


def cfg_full():
    return dict(D=4096, H=32, FF=11008, TPC=1024)


def cfg_tiny():
    return dict(D=512, H=4, FF=512, TPC=128)


def splits_of(n, k):
    base = n // k
    rem = n - base * k
    return [base + (1 if i < rem else 0) for i in range(k)]


def build_nc(cfg):
    D, H, FF, TPC = cfg["D"], cfg["H"], cfg["FF"], cfg["TPC"]
    DC = D // 128
    EB = D // 128
    HB = FF // 128
    TB = TPC // 2
    S = TPC * RANKS
    KT = S // 128
    KTL = TPC // 128
    LT = TPC // 128
    NQ = 2
    SP = splits_of(HB, NSPLIT)
    SO = [sum(SP[:i]) for i in range(NSPLIT)]
    SPMAX = max(SP)
    assert H == DC and TB <= 512 and D % 128 == 0 and FF % 128 == 0
    INV_SQRT_DK = float(1.0 / np.sqrt(128.0))
    EPS = 1e-5

    nc = bass.Bass("TRN2", target_bir_lowering=False)

    x_nat = nc.dram_tensor("x_nat", [TPC, D], F32, kind="ExternalInput").ap()
    xTd = nc.dram_tensor("xTd", [EB, 128, TPC], F32, kind="ExternalInput").ap()
    wq = nc.dram_tensor("wq", [EB, D, 128], F32, kind="ExternalInput").ap()
    wk = nc.dram_tensor("wk", [EB, D, 128], F32, kind="ExternalInput").ap()
    wv = nc.dram_tensor("wv", [EB, D, 128], F32, kind="ExternalInput").ap()
    wo = nc.dram_tensor("wo", [EB, D, 128], F32, kind="ExternalInput").ap()
    wg = nc.dram_tensor("wg", [HB, D, 128], I32, kind="ExternalInput").ap()
    wu = nc.dram_tensor("wu", [HB, D, 128], I32, kind="ExternalInput").ap()
    wd = nc.dram_tensor("wd", [EB, FF, 128], I32, kind="ExternalInput").ap()
    sg_d = nc.dram_tensor("sg", [1], F32, kind="ExternalInput").ap()
    su_d = nc.dram_tensor("su", [1], F32, kind="ExternalInput").ap()
    sd_d = nc.dram_tensor("sd", [1], F32, kind="ExternalInput").ap()
    outT = nc.dram_tensor("outT", [EB, 128, TPC], F32, kind="ExternalOutput").ap()

    ag_k_in = nc.dram_tensor("ag_k_in", [D, TPC], BF16, kind="Internal").ap()
    ag_v_in = nc.dram_tensor("ag_v_in", [D, TPC], BF16, kind="Internal").ap()
    agk_full = nc.dram_tensor("agk_full", [RANKS * D, TPC], BF16, kind="Internal").ap()
    agv_full = nc.dram_tensor("agv_full", [RANKS * D, TPC], BF16, kind="Internal").ap()
    xn_sp = nc.dram_tensor("xn_sp", [EB, 128, TPC], BF16, kind="Internal").ap()
    q_sp = nc.dram_tensor("q_sp", [EB, 128, TPC], BF16, kind="Internal").ap()
    y_sp = nc.dram_tensor("y_sp", [EB, 128, TPC], BF16, kind="Internal").ap()
    p_sp = nc.dram_tensor("p_sp", [EB, 128, TPC], F32, kind="Internal").ap()

    es = ExitStack()

    def sem(name):
        return es.enter_context(nc.semaphore(name))

    psum = es.enter_context(nc.psum_tensor("pspool", [128, 8, 512], F32)).ap()
    psum_bf = psum.bitcast(BF16)

    ident = es.enter_context(nc.sbuf_tensor("ident", [128, 128], BF16)).ap()
    flv = es.enter_context(nc.sbuf_tensor("flv", [1, 2], F32)).ap()
    fls = es.enter_context(nc.sbuf_tensor("fls", [1, 2], F32)).ap()
    scal = es.enter_context(nc.sbuf_tensor("scal", [128, 4], F32)).ap()
    onesk = es.enter_context(nc.sbuf_tensor("onesk", [128, 1], BF16)).ap()
    ones1 = es.enter_context(nc.sbuf_tensor("ones1", [1, 128], F32)).ap()

    xnT_st = ExitStack()
    xnT = xnT_st.enter_context(nc.sbuf_tensor("xnT", [128, DC, TPC], BF16)).ap()

    bar = sem("bar")
    bar_n = [0]
    blk = es.enter_context(nc.Block())
    ENGS = ["sync", "scalar", "vector", "tensor", "gpsimd"]

    def barrier():
        bar_n[0] += 1
        tgt = 5 * bar_n[0]

        @blk.vector
        def _(v):
            v.memset(flv[:, 0:1], 0.0).then_inc(bar, 1)
            v.wait_ge(bar, tgt)

        @blk.scalar
        def _(sc):
            sc.memzero(fls[:, 0:1]).then_inc(bar, 1)
            sc.wait_ge(bar, tgt)

        for e in ("sync", "tensor", "gpsimd"):
            @getattr(blk, e)
            def _(en):
                en.sem_inc(bar, 1)
                en.wait_ge(bar, tgt)

    s_init = sem("init")
    s_init2 = sem("init2")

    @blk.gpsimd
    def _(gp):
        gp.dma_start(out=scal[:, 0:1], in_=sg_d.unsqueeze(0).broadcast_to([128, 1])).then_inc(s_init, 16)
        gp.dma_start(out=scal[:, 2:3], in_=su_d.unsqueeze(0).broadcast_to([128, 1])).then_inc(s_init, 16)
        gp.dma_start(out=scal[:, 3:4], in_=sd_d.unsqueeze(0).broadcast_to([128, 1])).then_inc(s_init, 16)
        gp.memset(ident[:, :], 0.0).then_inc(s_init2, 1)
        gp.wait_ge(s_init, 48)
        gp.wait_ge(s_init2, 1)
        gp.affine_select(out=ident[:, :], in_=ident[:, :], compare_op=ALU.not_equal,
                         fill=1.0, base=0, pattern=[[-1, 128]], channel_multiplier=1).then_inc(s_init2, 1)
        gp.memset(onesk[:, :], 1.0).then_inc(s_init2, 1)
        gp.memset(ones1[:, :], 1.0).then_inc(s_init2, 1)
        gp.wait_ge(s_init2, 4)

    @blk.vector
    def _(v):
        v.wait_ge(s_init, 48)
        v.wait_ge(s_init2, 4)
        v.tensor_mul(scal[:, 1:2], scal[:, 2:3], scal[:, 3:4])

    barrier()

    ln_st = ExitStack()
    LNB = 4 if LT >= 4 else 2
    xtile = ln_st.enter_context(nc.sbuf_tensor("xtile", [128, LNB, D], F32)).ap()
    xnrow = ln_st.enter_context(nc.sbuf_tensor("xnrow", [128, LNB, D], BF16)).ap()
    sqj = ln_st.enter_context(nc.sbuf_tensor("sqj", [128, 2, D], BF16)).ap()
    stt = ln_st.enter_context(nc.sbuf_tensor("stt", [128, 2, 8], F32)).ap()

    s_xl = [sem(f"xl{i}") for i in range(LNB)]
    s_vch = sem("lnvch")
    s_v1, s_sq, s_s1, s_v2, s_ap, s_tr, s_te, s_sp = (
        sem("lnv1"), sem("lnsq"), sem("lns1"), sem("lnv2"),
        sem("lnap"), sem("lntr"), sem("lnte"), sem("lnsp"))

    GSZ = min(8, DC)
    NG = DC // GSZ

    @blk.sync
    def _(sy):
        for t in range(LT):
            if t >= LNB:
                sy.wait_ge(s_ap, t - LNB + 1)
            sy.dma_start(out=xtile[:, t % LNB, :], in_=x_nat[t * 128:(t + 1) * 128, :]).then_inc(s_xl[t % LNB], 16)
        sy.wait_ge(s_te, LT * NG)
        for eb in range(EB):
            sy.dma_start(out=xn_sp[eb], in_=xnT[:, eb, :]).then_inc(s_sp, 16)

    @blk.vector
    def _(v):
        nch = [0]

        def step(f):
            v.wait_ge(s_vch, nch[0])
            nch[0] += 1
            f().then_inc(s_vch, 1)
        for t in range(LT):
            v.wait_ge(s_xl[t % LNB], 16 * (t // LNB + 1))
            if t >= 2:
                v.wait_ge(s_ap, t - 1)
            xt = xtile[:, t % LNB, :]
            sp_ = stt[:, t % 2, :]
            step(lambda: v.reduce_sum(sp_[:, 0:1], xt, axis=AX.X))
            step(lambda: v.tensor_scalar_mul(sp_[:, 1:2], sp_[:, 0:1], 1.0 / D))
            v.wait_ge(s_sq, t + 1)
            step(lambda: v.tensor_mul(sp_[:, 3:4], sp_[:, 1:2], sp_[:, 1:2]))
            step(lambda: v.tensor_scalar_mul(sp_[:, 2:3], sp_[:, 2:3], 1.0 / D))
            step(lambda: v.tensor_sub(sp_[:, 3:4], sp_[:, 2:3], sp_[:, 3:4]))
            step(lambda: v.tensor_scalar_add(sp_[:, 3:4], sp_[:, 3:4], EPS))
            step(lambda: v.reciprocal(sp_[:, 4:5], sp_[:, 3:4]))
            v.wait_ge(s_s1, t + 1)
            step(lambda: v.tensor_mul(sp_[:, 6:7], sp_[:, 1:2], sp_[:, 5:6]))
            step(lambda: v.tensor_scalar_mul(sp_[:, 6:7], sp_[:, 6:7], -1.0))

    @blk.scalar
    def _(sc):
        for t in range(LT):
            sc.wait_ge(s_xl[t % LNB], 16 * (t // LNB + 1))
            if t >= 2:
                sc.wait_ge(s_vch, 9 * (t - 2) + 5)
            sc.activation(sqj[:, t % 2, :], xtile[:, t % LNB, :], AF.Square,
                          accum_out=stt[:, t % 2, 2:3]).then_inc(s_sq, 1)
            sc.wait_ge(s_vch, 9 * t + 7)
            sc.activation(stt[:, t % 2, 5:6], stt[:, t % 2, 4:5], AF.Sqrt).then_inc(s_s1, 1)
            sc.wait_ge(s_vch, 9 * t + 9)
            if t >= LNB:
                sc.wait_ge(s_tr, (t - LNB + 1) * NG)
            sc.activation(xnrow[:, t % LNB, :], xtile[:, t % LNB, :], AF.Identity,
                          bias=stt[:, t % 2, 6:7], scale=stt[:, t % 2, 5:6]).then_inc(s_ap, 1)
            for grp in range(NG):
                gi = t * NG + grp
                sc.wait_ge(s_tr, gi + 1)
                for j in range(GSZ):
                    dc = grp * GSZ + j
                    i = sc.copy(xnT[:, dc, t * 128:(t + 1) * 128],
                                psum_bf[:, gi % 2, j * 128:(j + 1) * 128])
                    if j == GSZ - 1:
                        i.then_inc(s_te, 1)

    @blk.tensor
    def _(tn):
        for t in range(LT):
            tn.wait_ge(s_ap, t + 1)
            for grp in range(NG):
                gi = t * NG + grp
                if gi >= 2:
                    tn.wait_ge(s_te, gi - 1)
                for j in range(GSZ):
                    dc = grp * GSZ + j
                    i = tn.transpose(psum_bf[:, gi % 2, j * 128:(j + 1) * 128],
                                     xnrow[:, t % LNB, dc * 128:(dc + 1) * 128], ident[:, :])
                    if j == GSZ - 1:
                        i.then_inc(s_tr, 1)

    barrier()
    ln_st.close()

    pj_st = ExitStack()
    PWB = 4
    wtf = pj_st.enter_context(nc.sbuf_tensor("wtf", [128, PWB, DC, 128], F32)).ap()
    wt = pj_st.enter_context(nc.sbuf_tensor("wt", [128, PWB, DC, 128], BF16)).ap()
    STGB = 8
    stg = pj_st.enter_context(nc.sbuf_tensor("stg", [128, STGB, TPC], BF16)).ap()

    s_wl = [sem(f"pwl{i}") for i in range(PWB)]
    s_wc = sem("pwc")
    s_mm = sem("pmm")
    s_ev = sem("pev")
    s_pst = [sem("pst0"), sem("pst1")]
    s_agk = sem("agk")
    s_agv = sem("agv")

    WCH = max(1, DC // 4)
    NW = (DC + WCH - 1) // WCH
    NCHK = min(4, EB, H)
    EBC = EB // NCHK
    HGC = H // NCHK

    kre = ag_k_in.rearrange("(eb p) t -> eb p t", p=128)
    vre = ag_v_in.rearrange("(eb p) t -> eb p t", p=128)
    agk_ch = agk_full.rearrange("(g r) t -> g r t", g=NCHK)
    agv_ch = agv_full.rearrange("(g r) t -> g r t", g=NCHK)
    projs = [("k", wk, kre), ("v", wv, vre), ("q", wq, q_sp)]
    NPJ = 3 * EB

    pst_cnt = [0, 0]
    pst_after = []
    for n in range(NPJ):
        pst_cnt[n % 2] += 16
        pst_after.append((pst_cnt[0], pst_cnt[1]))

    wsrcs = []
    for name, wsrc, dst in projs:
        for eb in range(EB):
            wsrcs.append((wsrc[eb], dst[eb]))

    def emit_wload(eng, n):
        if n >= PWB:
            eng.wait_ge(s_wc, n - PWB + 1)
        wr = wsrcs[n][0].rearrange("(dc p) e -> p dc e", p=128)
        for j in range(NW):
            eng.dma_start(out=wtf[:, n % PWB, j * WCH:(j + 1) * WCH, :],
                          in_=wr[:, j * WCH:(j + 1) * WCH, :]).then_inc(s_wl[n % PWB], 16)

    k_chunk_last = {(g + 1) * EBC - 1: g for g in range(NCHK)}
    v_chunk_last = {EB + (g + 1) * EBC - 1: g for g in range(NCHK)}

    @blk.gpsimd
    def _(gp):
        for m in range(NPJ):
            gp.wait_ge(s_ev, m + 1)
            gp.dma_start(out=wsrcs[m][1], in_=stg[:, m % STGB, :]).then_inc(s_pst[m % 2], 16)
            if m in k_chunk_last:
                g = k_chunk_last[m]
                gp.wait_ge(s_pst[0], pst_after[m][0])
                gp.wait_ge(s_pst[1], pst_after[m][1])
                gp.collective_compute("AllGather", ALU.bypass, replica_groups=RG,
                                      ins=[ag_k_in[g * EBC * 128:(g + 1) * EBC * 128, :]],
                                      outs=[agk_ch[g]]).then_inc(s_agk, 1)
            if m in v_chunk_last:
                g = v_chunk_last[m]
                gp.wait_ge(s_pst[0], pst_after[m][0])
                gp.wait_ge(s_pst[1], pst_after[m][1])
                gp.collective_compute("AllGather", ALU.bypass, replica_groups=RG,
                                      ins=[ag_v_in[g * EBC * 128:(g + 1) * EBC * 128, :]],
                                      outs=[agv_ch[g]]).then_inc(s_agv, 1)
        gp.wait_ge(s_pst[0], pst_after[NPJ - 1][0])
        gp.wait_ge(s_pst[1], pst_after[NPJ - 1][1])

    @blk.tensor
    def _(tn):
        for n in range(NPJ):
            tn.wait_ge(s_wc, n + 1)
            if n >= 2:
                tn.wait_ge(s_ev, n - 1)
            for dc in range(DC):
                first, last = dc == 0, dc == DC - 1
                b = (n % 2) * 2
                tn.matmul(psum[:, b, :TB], wt[:, n % PWB, dc, :], xnT[:, dc, 0:TB],
                          start=first, stop=last)
                i = tn.matmul(psum[:, b + 1, :TB], wt[:, n % PWB, dc, :], xnT[:, dc, TB:TPC],
                              start=first, stop=last)
                if last:
                    i.then_inc(s_mm, 1)

    @blk.vector
    def _(v):
        def cast(n):
            if n >= NPJ:
                return
            v.wait_ge(s_wl[n % PWB], 16 * NW * (n // PWB + 1))
            if n >= PWB:
                v.wait_ge(s_mm, n - PWB + 1)
            v.tensor_copy(wt[:, n % PWB, :, :], wtf[:, n % PWB, :, :]).then_inc(s_wc, 1)
        for k in range(PWB):
            cast(k)
        for n in range(NPJ):
            v.wait_ge(s_mm, n + 1)
            if n >= STGB:
                m = n - STGB
                v.wait_ge(s_pst[m % 2], 16 * (m // 2 + 1))
            b = (n % 2) * 2
            v.tensor_copy(stg[:, n % STGB, 0:TB], psum[:, b, :TB])
            v.tensor_copy(stg[:, n % STGB, TB:TPC], psum[:, b + 1, :TB]).then_inc(s_ev, 1)
            cast(n + PWB)

    @blk.sync
    def _(sy):
        for n in range(0, NPJ, 2):
            emit_wload(sy, n)
        sy.wait_ge(s_sp, 16 * EB)

    @blk.scalar
    def _(sc):
        for n in range(1, NPJ, 2):
            emit_wload(sc, n)

    barrier()
    pj_st.close()
    xnT_st.close()

    xnT2_st = ExitStack()
    xnT2 = xnT2_st.enter_context(nc.sbuf_tensor("xnT2", [128, DC, TPC], BF16)).ap()
    at_st1 = ExitStack()
    attnT = at_st1.enter_context(nc.sbuf_tensor("attnT", [128, DC, TPC], BF16)).ap()
    at_st = ExitStack()
    kt = at_st.enter_context(nc.sbuf_tensor("kt", [128, 2, S], BF16)).ap()
    VTB = 3
    vt = at_st.enter_context(nc.sbuf_tensor("vt", [128, VTB, KT, 128], BF16)).ap()
    qh = at_st.enter_context(nc.sbuf_tensor("qh", [128, 2, TPC], BF16)).ap()
    et = at_st.enter_context(nc.sbuf_tensor("et", [128, 2, NQ, TB], BF16)).ap()
    avf = at_st.enter_context(nc.sbuf_tensor("avf", [128, 2, NQ, TB], F32)).ap()
    rrow = at_st.enter_context(nc.sbuf_tensor("rrow", [1, 2, NQ, TB], F32)).ap()
    bcs = at_st.enter_context(nc.sbuf_tensor("bcs", [128, 2, NQ, TB], F32)).ap()
    rr_sp = nc.dram_tensor("rr_sp", [H, NQ, TB], F32, kind="Internal").ap()

    s_kl = [sem("akl0"), sem("akl1")]
    s_vl = [sem(f"avl{i}") for i in range(3)]
    s_ql = [sem("aql0"), sem("aql1")]
    s_smm = sem("asmm")
    s_exp = sem("aexp")
    s_dav = sem("adav")
    s_den = sem("aden")
    s_rc = sem("arc")
    s_rrs = [sem("arrs0"), sem("arrs1")]
    s_bcl = [sem("abcl0"), sem("abcl1")]
    s_avf = sem("aavf")
    s_aev = sem("aaev")
    s_x2 = sem("fx2")

    def vt_loads(eng, h):
        g = h // HGC
        lr = (h % HGC) * 128
        eng.wait_ge(s_agv, g + 1)
        if h >= VTB:
            eng.wait_ge(s_dav, h - 2)
        for r in range(RANKS):
            for j in range(KTL):
                ktg = r * KTL + j
                eng.dma_start_transpose(
                    vt[:, h % VTB, ktg, :],
                    agv_ch[g][r * EBC * 128 + lr: r * EBC * 128 + lr + 128, j * 128:(j + 1) * 128],
                ).then_inc(s_vl[h % VTB], 16)

    @blk.sync
    def _(sy):
        for h in range(H):
            if h % HGC == 0:
                g = h // HGC
                sy.wait_ge(s_agk, g + 1)
            if h >= 2:
                sy.wait_ge(s_smm, KT * (h - 1))
            g = h // HGC
            lr = (h % HGC) * 128
            for r in range(RANKS):
                sy.dma_start(out=kt[:, h % 2, r * TPC:(r + 1) * TPC],
                             in_=agk_ch[g][r * EBC * 128 + lr: r * EBC * 128 + lr + 128, :]).then_inc(s_kl[h % 2], 16)
            sy.dma_start(out=qh[:, h % 2, :], in_=q_sp[h]).then_inc(s_ql[h % 2], 16)
            vt_loads(sy, h)
            if h >= 1:
                hp = h - 1
                sy.wait_ge(s_rc, hp + 1)
                sy.dma_start(out=rr_sp[hp], in_=rrow[:, hp % 2, :, :]).then_inc(s_rrs[hp % 2], 16)
                sy.wait_ge(s_rrs[hp % 2], 16 * (hp // 2 + 1))
                sy.dma_start(out=bcs[:, hp % 2, :, :],
                             in_=rr_sp[hp].unsqueeze(0).broadcast_to([128, NQ, TB])).then_inc(s_bcl[hp % 2], 16)
        hp = H - 1
        sy.wait_ge(s_rc, hp + 1)
        sy.dma_start(out=rr_sp[hp], in_=rrow[:, hp % 2, :, :]).then_inc(s_rrs[hp % 2], 16)
        sy.wait_ge(s_rrs[hp % 2], 16 * (hp // 2 + 1))
        sy.dma_start(out=bcs[:, hp % 2, :, :],
                     in_=rr_sp[hp].unsqueeze(0).broadcast_to([128, NQ, TB])).then_inc(s_bcl[hp % 2], 16)
        sy.wait_ge(s_sp, 16 * EB)
        for eb in range(EB):
            sy.dma_start(out=xnT2[:, eb, :], in_=xn_sp[eb]).then_inc(s_x2, 16)

    @blk.tensor
    def _(tn):
        def emit_avo(h, ktj, nk):
            ep = ktj % 2
            tn.wait_ge(s_exp, nk + 1)
            if ktj == 0 and h >= 1:
                tn.wait_ge(s_avf, h)
                tn.wait_ge(s_rc, h)
            vtile = vt[:, h % VTB, ktj, :]
            tn.matmul(psum[:, 4, :TB], vtile, et[:, ep, 0, :],
                      start=(ktj == 0), stop=(ktj == KT - 1))
            i = tn.matmul(psum[:, 5, :TB], vtile, et[:, ep, 1, :],
                          start=(ktj == 0), stop=(ktj == KT - 1))
            if ktj == KT - 1:
                i.then_inc(s_dav, 1)
            tn.matmul(psum[0:1, 6, :TB], onesk[:, :], et[:, ep, 0, :],
                      start=(ktj == 0), stop=(ktj == KT - 1))
            i2 = tn.matmul(psum[0:1, 7, :TB], onesk[:, :], et[:, ep, 1, :],
                           start=(ktj == 0), stop=(ktj == KT - 1))
            if ktj == KT - 1:
                i2.then_inc(s_den, 1)

        nk = 0
        for h in range(H):
            tn.wait_ge(s_kl[h % 2], 16 * RANKS * (h // 2 + 1))
            tn.wait_ge(s_vl[h % VTB], 16 * KT * (h // VTB + 1))
            tn.wait_ge(s_ql[h % 2], 16 * (h // 2 + 1))
            for ktj in range(KT):
                if nk >= 2:
                    tn.wait_ge(s_exp, nk - 1)
                bS = (ktj % 2) * 2
                ktile = kt[:, h % 2, ktj * 128:(ktj + 1) * 128]
                tn.matmul(psum[:, bS, :TB], ktile, qh[:, h % 2, 0:TB], start=True, stop=True)
                tn.matmul(psum[:, bS + 1, :TB], ktile, qh[:, h % 2, TB:TPC],
                          start=True, stop=True).then_inc(s_smm, 1)
                if nk >= 1:
                    ph_, pk_ = divmod(nk - 1, KT)
                    emit_avo(ph_, pk_, nk - 1)
                nk += 1
        ph_, pk_ = divmod(nk - 1, KT)
        emit_avo(ph_, pk_, nk - 1)

    @blk.scalar
    def _(sc):
        nk = 0
        for h in range(H):
            for ktj in range(KT):
                sc.wait_ge(s_smm, nk + 1)
                bS = (ktj % 2) * 2
                sc.activation(et[:, ktj % 2, :, :], psum[:, bS:bS + 2, :TB], AF.Exp,
                              scale=INV_SQRT_DK).then_inc(s_exp, 1)
                nk += 1


    @blk.vector
    def _(v):
        for h in range(H):
            v.wait_ge(s_dav, h + 1)
            if h >= 2:
                v.wait_ge(s_aev, h - 1)
            v.tensor_copy(avf[:, h % 2, 0, :], psum[:, 4, :TB])
            v.tensor_copy(avf[:, h % 2, 1, :], psum[:, 5, :TB]).then_inc(s_avf, 1)
            v.wait_ge(s_den, h + 1)
            if h >= 2:
                v.wait_ge(s_rrs[h % 2], 16 * ((h - 2) // 2 + 1))
            v.reciprocal(rrow[:, h % 2, :, :], psum[0:1, 6:8, :TB]).then_inc(s_rc, 1)
            v.wait_ge(s_bcl[h % 2], 16 * (h // 2 + 1))
            v.wait_ge(s_avf, h + 1)
            v.tensor_mul(attnT[:, h, 0:TB], avf[:, h % 2, 0, :], bcs[:, h % 2, 0, :])
            v.tensor_mul(attnT[:, h, TB:TPC], avf[:, h % 2, 1, :], bcs[:, h % 2, 1, :]).then_inc(s_aev, 1)

    barrier()
    at_st.close()

    wo_st = ExitStack()
    wt2 = wo_st.enter_context(nc.sbuf_tensor("wt2", [128, 2, DC, 128], BF16)).ap()
    ystg = wo_st.enter_context(nc.sbuf_tensor("ystg", [128, 2, TPC], BF16)).ap()

    s_wl2 = [sem("wwl0"), sem("wwl1")]
    s_mm2 = sem("wmm")
    s_ev2 = sem("wev")
    s_yst = [sem("yst0"), sem("yst1")]

    @blk.gpsimd
    def _(gp):
        for ob in range(EB):
            if ob >= 2:
                gp.wait_ge(s_mm2, ob - 1)
            wr = wo[ob].rearrange("(dc p) e -> p dc e", p=128)
            for j in range(NW):
                gp.dma_start(out=wt2[:, ob % 2, j * WCH:(j + 1) * WCH, :],
                             in_=wr[:, j * WCH:(j + 1) * WCH, :]).then_inc(s_wl2[ob % 2], 16)

    @blk.tensor
    def _(tn):
        for ob in range(EB):
            tn.wait_ge(s_wl2[ob % 2], 16 * NW * (ob // 2 + 1))
            if ob >= 2:
                tn.wait_ge(s_ev2, ob - 1)
            for ec in range(DC):
                first, last = ec == 0, ec == DC - 1
                b = (ob % 2) * 2
                tn.matmul(psum[:, b, :TB], wt2[:, ob % 2, ec, :], attnT[:, ec, 0:TB],
                          start=first, stop=last)
                i = tn.matmul(psum[:, b + 1, :TB], wt2[:, ob % 2, ec, :], attnT[:, ec, TB:TPC],
                              start=first, stop=last)
                if last:
                    i.then_inc(s_mm2, 1)

    @blk.vector
    def _(v):
        for ob in range(EB):
            v.wait_ge(s_mm2, ob + 1)
            if ob >= 2:
                v.wait_ge(s_yst[ob % 2], 16 * (ob // 2))
            b = (ob % 2) * 2
            v.tensor_copy(ystg[:, ob % 2, 0:TB], psum[:, b, :TB])
            v.tensor_copy(ystg[:, ob % 2, TB:TPC], psum[:, b + 1, :TB]).then_inc(s_ev2, 1)

    @blk.sync
    def _(sy):
        for ob in range(EB):
            sy.wait_ge(s_ev2, ob + 1)
            sy.dma_start(out=y_sp[ob], in_=ystg[:, ob % 2, :]).then_inc(s_yst[ob % 2], 16)
        sy.wait_ge(s_yst[0], 16 * ((EB + 1) // 2))
        sy.wait_ge(s_yst[1], 16 * (EB // 2))

    barrier()
    wo_st.close()
    at_st1.close()

    ff_st = ExitStack()
    mid = ff_st.enter_context(nc.sbuf_tensor("mid", [128, SPMAX, TPC], BF16)).ap()
    wgu = ff_st.enter_context(nc.sbuf_tensor("wgu", [128, 2, 2, DC, 128], BF16)).ap()
    sil = ff_st.enter_context(nc.sbuf_tensor("sil", [128, TPC], BF16)).ap()
    wdb = ff_st.enter_context(nc.sbuf_tensor("wdb", [128, 2, SPMAX, 128], BF16)).ap()
    asm = ff_st.enter_context(nc.sbuf_tensor("asm", [128, TPC], F32)).ap()
    xTc = ff_st.enter_context(nc.sbuf_tensor("xTc", [128, 2, TPC], F32)).ap()
    yc = ff_st.enter_context(nc.sbuf_tensor("yc", [128, 2, TPC], BF16)).ap()
    p0c = ff_st.enter_context(nc.sbuf_tensor("p0c", [128, 2, TPC], F32)).ap()
    otg = ff_st.enter_context(nc.sbuf_tensor("otg", [128, 2, TPC], F32)).ap()

    s_ach = sem("fach")
    s_gwl = [sem("fgw0"), sem("fgw1")]
    s_gmm = sem("fgmm")
    s_sil = sem("fsil")
    s_mde = sem("fmde")
    s_dwl = [sem("fdw0"), sem("fdw1")]
    s_dmm = sem("fdmm")
    s_dev = sem("fdev")
    s_pl = [sem("fpl0"), sem("fpl1")]
    s_al = [sem("fal0"), sem("fal1")]
    s_fst = [sem("fst0"), sem("fst1")]

    WCH2 = max(1, SPMAX // 3)
    dwl_cnt = [0, 0]
    dwl_exp = {}
    def dwn_chunks(spr):
        ch = []
        lo = 0
        while lo < spr:
            hi = min(lo + WCH2, spr)
            ch.append((lo, hi))
            lo = hi
        return ch

    @blk.gpsimd
    def _(gp):
        for r in range(NSPLIT):
            for j in range(SP[r]):
                n = SO[r] + j
                hb = n
                if n >= 2:
                    gp.wait_ge(s_gmm, n - 1)
                for wi, wsrc in ((0, wg), (1, wu)):
                    wr = wsrc[hb].rearrange("(dc p) h -> p dc h", p=128)
                    for jj in range(NW):
                        gp.dma_start(out=wgu[:, n % 2, wi, jj * WCH:(jj + 1) * WCH, :],
                                     in_=wr[:, jj * WCH:(jj + 1) * WCH, :]).then_inc(s_gwl[n % 2], 16)
            nchk = dwn_chunks(SP[r])
            for ob in range(EB):
                m = r * EB + ob
                if m >= 2:
                    gp.wait_ge(s_dmm, m - 1)
                wr = wd[ob].rearrange("(hc p) o -> p hc o", p=128)[:, SO[r]:SO[r] + SP[r], :]
                for (lo, hi) in nchk:
                    gp.dma_start(out=wdb[:, m % 2, lo:hi, :],
                                 in_=wr[:, lo:hi, :]).then_inc(s_dwl[m % 2], 16)
                dwl_cnt[m % 2] += 16 * len(nchk)
                dwl_exp[m] = dwl_cnt[m % 2]

    @blk.tensor
    def _(tn):
        tn.wait_ge(s_x2, 16 * EB)
        for r in range(NSPLIT):
            nchk = dwn_chunks(SP[r])
            if r >= 1:
                tn.wait_ge(s_dev, r * EB)
            for j in range(SP[r]):
                n = SO[r] + j
                tn.wait_ge(s_gwl[n % 2], 16 * 2 * NW * (n // 2 + 1))
                if n >= 2:
                    tn.wait_ge(s_mde, n - 1)
                bg = (n % 2) * 4
                for dc in range(DC):
                    first, last = dc == 0, dc == DC - 1
                    tn.matmul(psum[:, bg + 0, :TB], wgu[:, n % 2, 0, dc, :], xnT2[:, dc, 0:TB], start=first, stop=last)
                    tn.matmul(psum[:, bg + 1, :TB], wgu[:, n % 2, 0, dc, :], xnT2[:, dc, TB:TPC], start=first, stop=last)
                    tn.matmul(psum[:, bg + 2, :TB], wgu[:, n % 2, 1, dc, :], xnT2[:, dc, 0:TB], start=first, stop=last)
                    i = tn.matmul(psum[:, bg + 3, :TB], wgu[:, n % 2, 1, dc, :], xnT2[:, dc, TB:TPC], start=first, stop=last)
                    if last:
                        i.then_inc(s_gmm, 1)
            tn.wait_ge(s_mde, SO[r] + SP[r])
            for ob in range(EB):
                m = r * EB + ob
                tn.wait_ge(s_dwl[m % 2], dwl_exp[m])
                if m >= 2:
                    tn.wait_ge(s_dev, m - 1)
                bd = (m % 2) * 4
                for hc in range(SP[r]):
                    first, last = hc == 0, hc == SP[r] - 1
                    tn.matmul(psum[:, bd, :TB], wdb[:, m % 2, hc, :], mid[:, hc, 0:TB], start=first, stop=last)
                    i = tn.matmul(psum[:, bd + 1, :TB], wdb[:, m % 2, hc, :], mid[:, hc, TB:TPC], start=first, stop=last)
                    if last:
                        i.then_inc(s_dmm, 1)

    @blk.scalar
    def _(sc):
        for r in range(NSPLIT):
            for j in range(SP[r]):
                n = SO[r] + j
                sc.wait_ge(s_gmm, n + 1)
                sc.wait_ge(s_mde, n)
                bg = (n % 2) * 4
                sc.activation(sil[:, 0:TB], psum[:, bg + 0, :TB], AF.Silu, scale=scal[:, 0:1])
                sc.activation(sil[:, TB:TPC], psum[:, bg + 1, :TB], AF.Silu, scale=scal[:, 0:1]).then_inc(s_sil, 1)

    @blk.vector
    def _(v):
        for r in range(NSPLIT):
            for j in range(SP[r]):
                n = SO[r] + j
                v.wait_ge(s_sil, n + 1)
                if j == 0 and r >= 1:
                    v.wait_ge(s_dmm, r * EB)
                bg = (n % 2) * 4
                v.tensor_mul(mid[:, j, 0:TB], sil[:, 0:TB], psum[:, bg + 2, :TB])
                v.tensor_mul(mid[:, j, TB:TPC], sil[:, TB:TPC], psum[:, bg + 3, :TB]).then_inc(s_mde, 1)
            for ob in range(EB):
                m = r * EB + ob
                v.wait_ge(s_dmm, m + 1)
                v.wait_ge(s_fst[m % 2], 16 * (m // 2))
                b = (m % 2) * 4
                if r == 0:
                    v.tensor_copy(otg[:, m % 2, 0:TB], psum[:, b, :TB])
                    v.tensor_copy(otg[:, m % 2, TB:TPC], psum[:, b + 1, :TB]).then_inc(s_dev, 1)
                elif r < NSPLIT - 1:
                    v.wait_ge(s_pl[ob % 2], 16 * ((r - 1) * EB // 2 + ob // 2 + 1))
                    v.tensor_add(otg[:, m % 2, 0:TB], psum[:, b, :TB], p0c[:, ob % 2, 0:TB])
                    v.tensor_add(otg[:, m % 2, TB:TPC], psum[:, b + 1, :TB], p0c[:, ob % 2, TB:TPC]).then_inc(s_dev, 1)
                else:
                    v.wait_ge(s_pl[ob % 2], 16 * ((r - 1) * EB // 2 + ob // 2 + 1))
                    v.wait_ge(s_al[ob % 2], 32 * (ob // 2 + 1))
                    v.tensor_add(asm[:, 0:TB], psum[:, b, :TB], p0c[:, ob % 2, 0:TB]).then_inc(s_ach, 1)
                    v.tensor_add(asm[:, TB:TPC], psum[:, b + 1, :TB], p0c[:, ob % 2, TB:TPC]).then_inc(s_ach, 1)
                    v.wait_ge(s_ach, 3 * ob + 2)
                    v.scalar_tensor_tensor(asm[:, :], asm[:, :], scal[:, 1:2], xTc[:, ob % 2, :],
                                           op0=ALU.mult, op1=ALU.add).then_inc(s_ach, 1)
                    v.wait_ge(s_ach, 3 * ob + 3)
                    v.tensor_add(otg[:, m % 2, :], asm[:, :], yc[:, ob % 2, :]).then_inc(s_dev, 1)

    @blk.sync
    def _(sy):
        for r in range(NSPLIT):
            if r == NSPLIT - 1:
                sy.wait_ge(s_yst[0], 16 * ((EB + 1) // 2))
                sy.wait_ge(s_yst[1], 16 * (EB // 2))
            for ob in range(EB):
                m = r * EB + ob
                if r >= 1:
                    if ob == 0:
                        sy.wait_ge(s_fst[0], 16 * (r * EB // 2))
                        sy.wait_ge(s_fst[1], 16 * (r * EB // 2))
                    if m >= 2:
                        sy.wait_ge(s_dev, m - 1)
                    sy.dma_start(out=p0c[:, ob % 2, :], in_=p_sp[ob]).then_inc(s_pl[ob % 2], 16)
                    if r == NSPLIT - 1:
                        sy.dma_start(out=xTc[:, ob % 2, :], in_=xTd[ob]).then_inc(s_al[ob % 2], 16)
                        sy.dma_start(out=yc[:, ob % 2, :], in_=y_sp[ob]).then_inc(s_al[ob % 2], 16)
                sy.wait_ge(s_dev, m + 1)
                dst = outT[ob] if r == NSPLIT - 1 else p_sp[ob]
                sy.dma_start(out=dst, in_=otg[:, m % 2, :]).then_inc(s_fst[m % 2], 16)
        tot = NSPLIT * EB
        sy.wait_ge(s_fst[0], 16 * ((tot + 1) // 2))
        sy.wait_ge(s_fst[1], 16 * (tot // 2))

    ff_st.close()
    xnT2_st.close()
    es.close()
    return nc



def block_w(w, nblk):
    o, i = w.shape
    assert o == nblk * 128
    return np.ascontiguousarray(w.T.reshape(i, nblk, 128).transpose(1, 0, 2))


def make_in_maps(cfg, x, wq, wk, wv, wo_, wg_, wu_, wd_, sg, su, sd):
    D, TPC = cfg["D"], cfg["TPC"]
    EB = D // 128
    HB = cfg["FF"] // 128
    wqb = block_w(wq, EB)
    wkb = block_w(wk, EB)
    wvb = block_w(wv, EB)
    wob = block_w(wo_, EB)
    wgb = block_w(wg_, HB)
    wub = block_w(wu_, HB)
    wdb_ = block_w(wd_, EB)
    maps = []
    for c in range(NCORE):
        b = c // RANKS
        lo = (c % RANKS) * TPC
        xs = np.ascontiguousarray(x[b, lo:lo + TPC])
        xT = np.ascontiguousarray(xs.T).reshape(EB, 128, TPC)
        maps.append(dict(
            x_nat=xs, xTd=xT, wq=wqb, wk=wkb, wv=wvb, wo=wob,
            wg=wgb, wu=wub, wd=wdb_,
            sg=np.asarray([sg], np.float32), su=np.asarray([su], np.float32),
            sd=np.asarray([sd], np.float32)))
    return maps


def assemble_out(cfg, results):
    D, TPC = cfg["D"], cfg["TPC"]
    S = TPC * RANKS
    out = np.empty((2, S, D), np.float32)
    for c in range(NCORE):
        b = c // RANKS
        lo = (c % RANKS) * TPC
        oT = results[c]["outT"].reshape(D, TPC)
        out[b, lo:lo + TPC] = oT.T
    return out




_CFG = cfg_full()
_NC = None


def _get_nc():
    global _NC
    if _NC is None:
        _NC = build_nc(_CFG)
    return _NC


def kernel(L0_attn_x, L0_attn_w_q, L0_attn_w_k, L0_attn_w_v, L0_attn_w_o,
           L0_ffn_w_gate, L0_ffn_w_up, L0_ffn_w_down,
           L0_ffn_scale_gate, L0_ffn_scale_up, L0_ffn_scale_down):
    cfg = _CFG
    x = np.asarray(L0_attn_x, dtype=np.float32)
    maps = make_in_maps(
        cfg, x,
        np.asarray(L0_attn_w_q, dtype=np.float32),
        np.asarray(L0_attn_w_k, dtype=np.float32),
        np.asarray(L0_attn_w_v, dtype=np.float32),
        np.asarray(L0_attn_w_o, dtype=np.float32),
        np.asarray(L0_ffn_w_gate, dtype=np.int32),
        np.asarray(L0_ffn_w_up, dtype=np.int32),
        np.asarray(L0_ffn_w_down, dtype=np.int32),
        float(np.asarray(L0_ffn_scale_gate)),
        float(np.asarray(L0_ffn_scale_up)),
        float(np.asarray(L0_ffn_scale_down)))
    nc = _get_nc()
    res = run_bass_kernel_spmd(nc, maps, core_ids=list(range(NCORE)))
    return assemble_out(cfg, [r for r in res.results])
